# version 19
# speedup vs baseline: 1.1382x; 1.0154x over previous
from contextlib import ExitStack

import numpy as np
import ml_dtypes

import concourse.bass as bass
import concourse.mybir as mybir
from concourse.bass_utils import run_bass_kernel_spmd
from concourse.bass_core_rust import IndirectOffsetOnAxis

F32 = mybir.dt.float32
BF16 = mybir.dt.bfloat16
U32 = mybir.dt.uint32
F16 = mybir.dt.float16
AF = mybir.ActivationFunctionType
ALU = mybir.AluOpType
AX = mybir.AxisListType

T = 8192
D = 1024
OWN = 2048
CAP = 1024
NE = 8
VD = 128
EPS_ATTN = float(np.finfo(np.float32).eps)
EPS_LN = 1e-5

ENGS = ("sync", "tensor", "vector", "scalar", "gpsimd")
UPTO = 99
P2_PAIRS = 2
P2_QCMAX = 16
P2_GATING = True
P2_CORE = True
P2_EPI = True
P2_CF = True
P2_DMM = True
P2_EPI_DROW = True
P2_EPI_SSQM = True
P2_EPI_SQ = True
P2_EPI_Y = True
P4_A = True
P4_B = True
TAPS = []
RDBG = False


class Sem:
    def __init__(self, nc, ctx, name):
        self.h = ctx.enter_context(nc.semaphore(name))
        self.n = 0

    def bump(self, k=1):
        self.n += k
        return self.n

    def tok(self):
        return (self, self.n)


class Plan:
    def __init__(self, k):
        self.k = k
        self.q = {e: [] for e in ENGS}

    def add(self, eng, fn):
        self.q[eng].append(fn)

    def tw(self, eng, toks):
        if isinstance(toks, tuple):
            toks = [toks]
        for sem, cnt in toks:
            if cnt > 0:
                self.q[eng].append(lambda e, s=sem.h, c=cnt: e.wait_ge(s, c))

    def slot(self, eng):
        pool = self.k.dpool[eng]
        i = self.k.dctr[eng] % len(pool)
        self.k.dctr[eng] += 1
        sem = pool[i]
        if sem.n > 0:
            self.q[eng].append(lambda e, s=sem.h, c=sem.n: e.wait_ge(s, c))
        cnt = sem.bump(16)
        return (sem, cnt)

    def dma(self, eng, out, in_, transpose=False):
        sem, cnt = self.slot(eng)
        if transpose:
            self.q[eng].append(lambda e, o=out, i=in_, s=sem.h:
                               e.dma_start(out=o, in_=i, transpose=True).then_inc(s, 16))
        else:
            self.q[eng].append(lambda e, o=out, i=in_, s=sem.h:
                               e.dma_start(out=o, in_=i).then_inc(s, 16))
        return (sem, cnt)

    def qdrain(self, eng):
        return [(s, s.n) for s in self.k.dpool[eng] if s.n > 0]

    def alldrain(self):
        out = []
        for e in ("sync", "scalar", "gpsimd", "vector"):
            out += self.qdrain(e)
        return out

    def emit(self):
        for eng in ("sync", "scalar", "gpsimd", "vector"):
            self.tw(eng, self.qdrain(eng))
        with self.k.nc.Block() as blk:
            for name in ENGS:
                fns = self.q[name]
                if fns:
                    getattr(blk, name)(lambda eng, fns=fns: [f(eng) for f in fns])


class Guard:
    def __init__(self):
        self.m = {}

    def use(self, p, weng, key):
        if key in self.m:
            p.tw(weng, self.m[key])

    def consumed(self, key, tok):
        self.m[key] = tok


class K:
    pass


def build_nc():
    nc = bass.Bass("TRN2", target_bir_lowering=False, debug=False)
    k = K()
    k.nc = nc
    dt = nc.dram_tensor

    k.xb = dt("xb", [T, D], F32, kind="ExternalInput").ap()
    k.xown = dt("xown", [OWN, D], F32, kind="ExternalInput").ap()
    k.wq = dt("wq", [D, 256], F32, kind="ExternalInput").ap()
    k.wk = dt("wk", [D, 256], F32, kind="ExternalInput").ap()
    k.wv = dt("wv", [D, 512], F32, kind="ExternalInput").ap()
    k.wg = dt("wg", [D, 512], F32, kind="ExternalInput").ap()
    k.bqc = dt("bqc", [128, 2], F32, kind="ExternalInput").ap()
    k.bkc = dt("bkc", [128, 2], F32, kind="ExternalInput").ap()
    k.bgc = dt("bgc", [128, 4], F32, kind="ExternalInput").ap()
    k.bvr = dt("bvr", [1, 512], F32, kind="ExternalInput").ap()
    k.wo = dt("wo", [2048, D], F32, kind="ExternalInput").ap()
    k.bor = dt("bor", [1, D], F32, kind="ExternalInput").ap()
    k.gwT = dt("gwT", [D, 8], F32, kind="ExternalInput").ap()
    k.w1T = dt("w1T", [NE, D, D], F32, kind="ExternalInput").ap()
    k.w2T = dt("w2T", [NE, D, D], F32, kind="ExternalInput").ap()
    k.w3T = dt("w3T", [NE, D, D], F32, kind="ExternalInput").ap()
    k.b1c = dt("b1c", [NE, 128, 8], F32, kind="ExternalInput").ap()
    k.b2r = dt("b2r", [NE, D], F32, kind="ExternalInput").ap()
    k.b3r = dt("b3r", [NE, D], F32, kind="ExternalInput").ap()
    k.sw1T = dt("sw1T", [D, 2048], F32, kind="ExternalInput").ap()
    k.sw2T = dt("sw2T", [D, 2048], F32, kind="ExternalInput").ap()
    k.sw3T = dt("sw3T", [2048, D], F32, kind="ExternalInput").ap()
    k.qrowt = dt("qrowt", [2, 128, T], BF16, kind="ExternalInput").ap()
    k.kcolt = dt("kcolt", [2, 128, T], BF16, kind="ExternalInput").ap()
    k.cft = dt("cft", [128, 2176], F32, kind="ExternalInput").ap()
    k.tri4 = dt("tri4", [4, 128, 512], BF16, kind="ExternalInput").ap()
    k.ccidx = dt("ccidx", [128, 16], U32, kind="ExternalInput").ap()
    k.iot6 = dt("iot6", [128, 8], F32, kind="ExternalInput").ap()
    k.iot0 = dt("iot0", [128, 8], F32, kind="ExternalInput").ap()
    k.tokf = dt("tokf", [128, 16], F32, kind="ExternalInput").ap()
    k.trsf = dt("trsf", [128, 1], F32, kind="ExternalInput").ap()
    k.eioc = dt("eioc", [8, 1], F32, kind="ExternalInput").ap()
    k.dmpp = dt("dmpp", [128, 1], F32, kind="ExternalInput").ap()

    k.out = dt("out", [OWN, D], F32, kind="ExternalOutput").ap()

    k.XT1 = dt("XT1", [T, D], BF16).ap()
    k.QT = dt("QT", [256, T], BF16).ap()
    k.KT = dt("KT", [256, T], BF16).ap()
    k.VT = dt("VT", [T, 512], BF16).ap()
    k.GT = dt("GT", [512, T], BF16).ap()
    k.SCRD = dt("SCRD", [4, T], F32).ap()
    k.SCRS = dt("SCRS", [2, T], F32).ap()
    k.CCI = dt("CCI", [512, T], BF16).ap()
    k.CCO = dt("CCO", [2048, T], BF16).ap()
    k.X2S = dt("X2S", [OWN, D], F32).ap()
    k.XT2 = dt("XT2", [OWN, D], BF16).ap()
    k.RID = dt("RID", [2, OWN], F16).ap()
    k.SLOTD = dt("SLOTD", [2, OWN], U32).ap()
    k.RNKD = dt("RNKD", [2, OWN], F32).ap()
    k.XSC = dt("XSC", [NE * CAP + 128, D], BF16).ap()
    k.YZS = dt("YZS", [OWN + 128, D], F32).ap()
    k.SLOTTOK = dt("sltk", [NE * CAP + 128, 1], U32, kind="ExternalInput").ap()
    k.RDB = dt("RDB", [128, 512], F32).ap()

    k.taps = {}
    for name, rows, cols in TAPS:
        k.taps[name] = dt(f"tap_{name}", [rows, cols],
                          getattr(k, name).dtype, kind="ExternalOutput").ap()

    ctx = ExitStack()
    k.ctx = ctx
    E = ctx.enter_context

    k.dpool = {
        "sync": [Sem(nc, ctx, f"qd{i}") for i in range(10)],
        "scalar": [Sem(nc, ctx, f"sd{i}") for i in range(10)],
        "gpsimd": [Sem(nc, ctx, f"gd{i}") for i in range(6)],
        "vector": [Sem(nc, ctx, f"vd{i}") for i in range(4)],
    }
    k.dctr = {"sync": 0, "scalar": 0, "gpsimd": 0, "vector": 0}
    k.cc = Sem(nc, ctx, "cc")
    k.te = Sem(nc, ctx, "te")
    k.ve = Sem(nc, ctx, "ve")
    k.se = Sem(nc, ctx, "se")
    k.ge = Sem(nc, ctx, "ge")

    k.onesr = E(nc.sbuf_tensor("onesr", [1, 128], BF16))
    k.onesrf = E(nc.sbuf_tensor("onesrf", [1, 128], F32))
    k.onescb = E(nc.sbuf_tensor("onescb", [128, 1], BF16))
    k.onescf = E(nc.sbuf_tensor("onescf", [8, 1], F32))
    k.onescfh = E(nc.sbuf_tensor("onescfh", [8, 1], F16))
    k.bqcs = E(nc.sbuf_tensor("bqcs", [128, 2], F32))
    k.bkcs = E(nc.sbuf_tensor("bkcs", [128, 2], F32))
    k.bgcs = E(nc.sbuf_tensor("bgcs", [128, 4], F32))
    k.b1cs = E(nc.sbuf_tensor("b1cs", [128, NE, 8], F32))
    k.gwTs = E(nc.sbuf_tensor("gwTs", [128, 8, 8], BF16))
    k.ccidxs = E(nc.sbuf_tensor("ccidxs", [128, 16], U32))

    k.ps = [E(nc.psum_tensor(f"ps{i}", [128, 512], F32)) for i in range(8)]
    k.slotc = [E(nc.sbuf_tensor(f"p4sl{i}", [128, 16], U32)) for i in range(2)]

    phase0(k)
    if UPTO >= 1:
        phase1(k)
    if UPTO >= 2:
        phase2(k)
    if UPTO >= 3:
        phase3(k)
    if UPTO >= 4:
        phase4(k)
    if TAPS:
        phase_taps(k)
    ctx.close()
    return nc


def phase_taps(k):
    nc = k.nc
    with ExitStack() as ectx:
        buf = ectx.enter_context(nc.sbuf_tensor("tapbuf", [128, 2048], F32))
        p = Plan(k)
        for name, rows, cols in TAPS:
            src = getattr(k, name)
            dst = k.taps[name]
            bv = buf.ap().bitcast(src.dtype)
            for r0 in range(0, rows, 128):
                nr = min(128, rows - r0)
                ld = p.dma("sync", bv[0:nr, 0:cols], src[r0:r0 + nr, 0:cols])
                p.tw("sync", ld)
                p.dma("sync", dst[r0:r0 + nr, :], bv[0:nr, 0:cols])
        p.tw("sync", p.qdrain("sync"))
        p.emit()


def phase0(k):
    nc = k.nc
    ve, se, ge = k.ve, k.se, k.ge
    with ExitStack() as ectx:
        E = ectx.enter_context
        p = Plan(k)

        def vset(v, s=ve.h):
            v.memset(k.onesr[:], 1.0).then_inc(s, 1)
            v.memset(k.onesrf[:], 1.0).then_inc(s, 1)
            v.memset(k.onescb[:], 1.0).then_inc(s, 1)
            v.memset(k.onescf[:], 1.0).then_inc(s, 1)
            v.memset(k.onescfh[:], 1.0).then_inc(s, 1)
        p.add("vector", vset)
        ve.bump(5)
        k.ones_ready = ve.tok()

        p.dma("sync", k.bqcs[:], k.bqc)
        p.dma("sync", k.bkcs[:], k.bkc)
        p.dma("sync", k.bgcs[:], k.bgc)
        p.dma("sync", k.b1cs.ap(), k.b1c.rearrange("e p t -> p e t"))
        p.dma("sync", k.ccidxs[:], k.ccidx)
        k.gw_ready = p.dma("gpsimd", k.gwTs.ap(),
                           k.gwT.rearrange("(c p) e -> p c e", p=128))

        k.p0_done = p.alldrain()

        p.emit()


def ln_cols(k, p, src_ap, cols, base, sqdump, n_feat=D):
    ve, se = k.ve, k.se
    s1 = cols[:, base + 0:base + 1]
    sq = cols[:, base + 1:base + 2]
    rstd = cols[:, base + 2:base + 3]
    nmr = cols[:, base + 3:base + 4]

    def v0(v, a=src_ap, o=s1, s=ve.h):
        v.tensor_reduce(o, a, AX.X, ALU.add).then_inc(s, 1)
    p.add("vector", v0)
    c_v0 = ve.bump()
    def s0(s_, a=src_ap, o=sqdump, acc=sq, s=se.h):
        s_.activation(o, a, AF.Square, accum_out=acc).then_inc(s, 1)
    p.add("scalar", s0)
    c_sq = se.bump()

    inv = 1.0 / n_feat
    def v1(v, s1=s1, sq=sq, rstd=rstd, nmr=nmr, c=c_sq, cv=c_v0, base=ve.n,
           s=ve.h):
        v.wait_ge(se.h, c)
        v.wait_ge(s, cv)
        v.tensor_scalar(nmr, s1, inv, None, ALU.mult).then_inc(s, 1)
        v.scalar_tensor_tensor(s1, s1, inv * inv, s1, ALU.mult,
                               ALU.mult).then_inc(s, 1)
        v.tensor_scalar(rstd, sq, inv, EPS_LN, ALU.mult, ALU.add).then_inc(s, 1)
        v.wait_ge(s, base + 3)
        v.scalar_tensor_tensor(rstd, s1, -1.0, rstd, ALU.mult,
                               ALU.add).then_inc(s, 1)
    p.add("vector", v1)
    ve.bump(3)
    c_v1 = ve.bump()
    def s1f(s_, rstd=rstd, sq=sq, c=c_v1, s=se.h):
        s_.wait_ge(k.ve.h, c)
        s_.activation(sq, rstd, AF.Sqrt).then_inc(s, 1)
    p.add("scalar", s1f)
    c_s2 = se.bump()
    def v2(v, sq=sq, rstd=rstd, nmr=nmr, c=c_s2, cv=c_v1, base=ve.n, s=ve.h):
        v.wait_ge(k.se.h, c)
        v.wait_ge(s, cv)
        v.reciprocal(rstd, sq).then_inc(s, 1)
        v.wait_ge(s, base + 1)
        v.scalar_tensor_tensor(nmr, nmr, -1.0, rstd, ALU.mult,
                               ALU.mult).then_inc(s, 1)
    p.add("vector", v2)
    ve.bump(2)
    return rstd, nmr, ve.n


def phase1(k):
    nc = k.nc
    te, ve, se, ge = k.te, k.ve, k.se, k.ge
    with ExitStack() as ectx:
        E = ectx.enter_context
        wqS = E(nc.sbuf_tensor("wqS", [128, 8, 256], BF16))
        wkS = E(nc.sbuf_tensor("wkS", [128, 8, 256], BF16))
        wvS = E(nc.sbuf_tensor("wvS", [128, 8, 512], BF16))
        wgS = E(nc.sbuf_tensor("wgS", [128, 8, 512], BF16))
        bvrow = E(nc.sbuf_tensor("bvrow", [1, 512], BF16))
        xt = [E(nc.sbuf_tensor(f"p1x{i}", [128, D], F32)) for i in range(4)]
        xnt = [E(nc.sbuf_tensor(f"p1xn{i}", [128, D], BF16)) for i in range(4)]
        sqd = E(nc.sbuf_tensor("p1sqd", [128, D], BF16))
        cols = E(nc.sbuf_tensor("p1cols", [128, 32], F32))
        xtT = [E(nc.sbuf_tensor(f"p1tT{c}", [128, 4096], BF16)) for c in range(8)]
        ptmp = [E(nc.sbuf_tensor(f"p1t{i}", [128, 512], BF16)) for i in range(2)]
        pout = [E(nc.sbuf_tensor(f"p1o{i}", [128, 512], BF16)) for i in range(2)]
        rowt = [E(nc.sbuf_tensor(f"p1rt{i}", [128, 512], BF16)) for i in range(2)]

        p = Plan(k)
        G = Guard()

        wtoks = [
            p.dma("gpsimd", wqS.ap(), k.wq.rearrange("(c p) f -> p c f", p=128)),
            p.dma("gpsimd", wkS.ap(), k.wk.rearrange("(c p) f -> p c f", p=128)),
            p.dma("gpsimd", wvS.ap(), k.wv.rearrange("(c p) f -> p c f", p=128)),
            p.dma("gpsimd", wgS.ap(), k.wg.rearrange("(c p) f -> p c f", p=128)),
            p.dma("gpsimd", bvrow[:], k.bvr),
        ]

        inv = 1.0 / D
        for t2 in range(T // 256):
            psl = t2 % 4
            base = 8 * psl
            s1_2 = cols[:, base + 0:base + 2]
            sq_2 = cols[:, base + 2:base + 4]
            rstd_2 = cols[:, base + 4:base + 6]
            nmr_2 = cols[:, base + 6:base + 8]
            G.use(p, "vector", ("cols", psl))
            bs = []
            for i in range(2):
                tt = 2 * t2 + i
                b = tt % 4
                bs.append(b)
                lq = "scalar"
                G.use(p, lq, ("xt", b))
                lc = p.dma(lq, xt[b][:], k.xb[tt * 128:(tt + 1) * 128, :])
                p.tw("vector", lc)
                p.tw("scalar", lc)
                def v0(v, a=xt[b][:], o=cols[:, base + i:base + i + 1], s=ve.h):
                    v.tensor_reduce(o, a, AX.X, ALU.add).then_inc(s, 1)
                p.add("vector", v0)
                ve.bump()
                def s0(s_, a=xt[b][:], acc=cols[:, base + 2 + i:base + 3 + i],
                       s=se.h):
                    s_.activation(sqd[:], a, AF.Square, accum_out=acc).then_inc(s, 1)
                p.add("scalar", s0)
                se.bump()
            c_v0 = ve.n
            c_sq = se.n
            def v1(v, s1=s1_2, sq=sq_2, rstd=rstd_2, nmr=nmr_2, c=c_sq,
                   cv=c_v0, vb=ve.n, s=ve.h):
                v.wait_ge(se.h, c)
                v.wait_ge(s, cv)
                v.tensor_scalar(nmr, s1, inv, None, ALU.mult).then_inc(s, 1)
                v.scalar_tensor_tensor(s1, s1, inv * inv, s1, ALU.mult,
                                       ALU.mult).then_inc(s, 1)
                v.tensor_scalar(rstd, sq, inv, EPS_LN, ALU.mult,
                                ALU.add).then_inc(s, 1)
                v.wait_ge(s, vb + 3)
                v.scalar_tensor_tensor(rstd, s1, -1.0, rstd, ALU.mult,
                                       ALU.add).then_inc(s, 1)
            p.add("vector", v1)
            ve.bump(3)
            c_v1 = ve.bump()
            def s1f(s_, rstd=rstd_2, sq=sq_2, c=c_v1, s=se.h):
                s_.wait_ge(ve.h, c)
                s_.activation(sq, rstd, AF.Sqrt).then_inc(s, 1)
            p.add("scalar", s1f)
            c_s2 = se.bump()
            def v2(v, sq=sq_2, rstd=rstd_2, nmr=nmr_2, c=c_s2, vb=ve.n, s=ve.h):
                v.wait_ge(se.h, c)
                v.reciprocal(rstd, sq).then_inc(s, 1)
                v.wait_ge(s, vb + 1)
                v.scalar_tensor_tensor(nmr, nmr, -1.0, rstd, ALU.mult,
                                       ALU.mult).then_inc(s, 1)
            p.add("vector", v2)
            ve.bump()
            c_cols = ve.bump()
            for i in range(2):
                tt = 2 * t2 + i
                b = bs[i]
                G.use(p, "scalar", ("xn", b))
                def sx(s_, a=xt[b][:], o=xnt[b][:],
                       sc=cols[:, base + 4 + i:base + 5 + i],
                       bi=cols[:, base + 6 + i:base + 7 + i], c=c_cols, s=se.h):
                    s_.wait_ge(ve.h, c)
                    s_.activation(o, a, AF.Identity, bias=bi, scale=sc).then_inc(s, 1)
                p.add("scalar", sx)
                c_xn = se.bump()
                G.consumed(("xt", b), (se, c_xn))
                p.tw("sync", (se, c_xn))
                st = p.dma("sync", k.XT1[tt * 128:(tt + 1) * 128, :], xnt[b][:])
                G.consumed(("xn", b), st)
                if i == 1:
                    G.consumed(("cols", psl), (se, c_xn))

        for half in range(2):
            ln_drain = p.qdrain("sync")
            for dc in range(8):
                G.use(p, "sync", ("xtT", dc))
                if dc == 0:
                    p.tw("sync", ln_drain)
                tokT = p.dma("sync", xtT[dc][:],
                             k.XT1[half * 4096:(half + 1) * 4096,
                                   dc * 128:(dc + 1) * 128], transpose=True)
                G.consumed(("xtT_ld", dc), tokT)
            xT_toks = [G.m[("xtT_ld", dc)] for dc in range(8)]
            for tc in range(8):
                gtok = half * 4096 + tc * 512

                def qk_proj(wS, bcs, rowtab, dstT, ft, bank_i):
                    bank = k.ps[bank_i]
                    G.use(p, "scalar", ("rowt", ft))
                    rt_tok = p.dma("scalar", rowt[ft][:], rowtab[ft, :, gtok:gtok + 512])
                    G.use(p, "tensor", ("bank", bank_i))
                    p.tw("tensor", wtoks + xT_toks)
                    def mm(t, wS=wS, ft=ft, tc=tc, bank=bank, s=te.h):
                        for dc in range(8):
                            ins = t.matmul(bank[:, :], wS[:, dc, ft * 128:(ft + 1) * 128],
                                           xtT[dc][:, tc * 512:(tc + 1) * 512],
                                           start=(dc == 0), stop=(dc == 7))
                        ins.then_inc(s, 1)
                    p.add("tensor", mm)
                    c_mm = te.bump()
                    G.use(p, "scalar", ("ptmp", ft))
                    def ev(s_, bank=bank, o=ptmp[ft][:], bi=bcs[:, ft:ft + 1], c=c_mm,
                           s=se.h):
                        s_.wait_ge(te.h, c)
                        s_.activation(o, bank[:, :], AF.Identity, bias=bi).then_inc(s, 1)
                    p.add("scalar", ev)
                    c_ev = se.bump()
                    G.consumed(("bank", bank_i), (se, c_ev))
                    G.use(p, "vector", ("pout", ft))
                    p.tw("vector", rt_tok)
                    def sc(v, a=ptmp[ft][:], rt=rowt[ft][:], o=pout[ft][:],
                           cc1=c_ev, s=ve.h):
                        v.wait_ge(se.h, cc1)
                        v.tensor_tensor(o, a, rt, ALU.mult).then_inc(s, 1)
                    p.add("vector", sc)
                    c_sc = ve.bump()
                    G.consumed(("ptmp", ft), (ve, c_sc))
                    G.consumed(("rowt", ft), (ve, c_sc))
                    p.tw("gpsimd", (ve, c_sc))
                    stc = p.dma("gpsimd", dstT[ft * 128:(ft + 1) * 128, gtok:gtok + 512],
                                pout[ft][:])
                    G.consumed(("pout", ft), stc)

                for ft in range(2):
                    qk_proj(wqS, k.bqcs, k.qrowt, k.QT, ft, ft)
                for ft in range(2):
                    qk_proj(wkS, k.bkcs, k.kcolt, k.KT, ft, ft)
                for ft in range(4):
                    bi_ = 2 + (ft % 2)
                    bank = k.ps[bi_]
                    G.use(p, "tensor", ("bank", bi_))
                    p.tw("tensor", wtoks + xT_toks)
                    def mmg(t, ft=ft, tc=tc, bank=bank, s=te.h):
                        for dc in range(8):
                            ins = t.matmul(bank[:, :], wgS[:, dc, ft * 128:(ft + 1) * 128],
                                           xtT[dc][:, tc * 512:(tc + 1) * 512],
                                           start=(dc == 0), stop=(dc == 7))
                        ins.then_inc(s, 1)
                    p.add("tensor", mmg)
                    c_mm = te.bump()
                    G.use(p, "scalar", ("ptmp", ft % 2))
                    def evg(s_, bank=bank, o=ptmp[ft % 2][:], bi=k.bgcs[:, ft:ft + 1],
                            c=c_mm, s=se.h):
                        s_.wait_ge(te.h, c)
                        s_.activation(o, bank[:, :], AF.Identity, bias=bi).then_inc(s, 1)
                    p.add("scalar", evg)
                    c_ev = se.bump()
                    G.consumed(("bank", bi_), (se, c_ev))
                    p.tw("gpsimd", (se, c_ev))
                    stc = p.dma("gpsimd", k.GT[ft * 128:(ft + 1) * 128, gtok:gtok + 512],
                                ptmp[ft % 2][:])
                    G.consumed(("ptmp", ft % 2), stc)
                for c4_ in range(4):
                    bi_ = 4 + (c4_ % 2)
                    bank = k.ps[bi_]
                    G.use(p, "tensor", ("bank", bi_))
                    p.tw("tensor", wtoks + xT_toks + [k.ones_ready])
                    def mmv(t, c4_=c4_, tc=tc, bank=bank, s=te.h):
                        for dc in range(8):
                            t.matmul(bank[:, :],
                                     xtT[dc][:, tc * 512 + c4_ * 128:tc * 512 + (c4_ + 1) * 128],
                                     wvS[:, dc, :], start=(dc == 0), stop=False)
                        t.matmul(bank[:, :], k.onesr[0:1, :], bvrow[0:1, :],
                                 start=False, stop=True).then_inc(s, 1)
                    p.add("tensor", mmv)
                    c_mm = te.bump()
                    G.use(p, "vector", ("pout", c4_ % 2))
                    def evv(v, bank=bank, o=pout[c4_ % 2][:], c=c_mm, s=ve.h):
                        v.wait_ge(te.h, c)
                        v.tensor_copy(o, bank[:, :]).then_inc(s, 1)
                    p.add("vector", evv)
                    c_ev = ve.bump()
                    G.consumed(("bank", bi_), (ve, c_ev))
                    p.tw("gpsimd", (ve, c_ev))
                    stc = p.dma("gpsimd", k.VT[gtok + c4_ * 128:gtok + (c4_ + 1) * 128, :],
                                pout[c4_ % 2][:])
                    G.consumed(("pout", c4_ % 2), stc)
            for dc in range(8):
                G.consumed(("xtT", dc), (te, te.n))
        k.p1_done = p.qdrain("sync") + p.qdrain("gpsimd")
        p.emit()


def phase2(k):
    nc = k.nc
    te, ve, se, ge = k.te, k.ve, k.se, k.ge
    with ExitStack() as ectx:
        E = ectx.enter_context
        qs = E(nc.sbuf_tensor("p2q", [128, T], BF16))
        ks = E(nc.sbuf_tensor("p2k", [128, T], BF16))
        vs = E(nc.sbuf_tensor("p2v", [128, 64, 256], BF16))
        Ssb = [[E(nc.sbuf_tensor(f"p2S{h}_{i}", [128, 512], BF16)) for i in range(2)]
               for h in range(2)]
        Sab = [[E(nc.sbuf_tensor(f"p2A{h}_{i}", [128, 512], BF16)) for i in range(2)]
               for h in range(2)]
        Srw = [E(nc.sbuf_tensor(f"p2R{h}", [128, 512], BF16)) for h in range(2)]
        r2 = [E(nc.sbuf_tensor(f"p2r2{h}", [128, 512], BF16)) for h in range(2)]
        yraw = [E(nc.sbuf_tensor(f"p2y{h}", [128, T], BF16)) for h in range(2)]
        drow = E(nc.sbuf_tensor("p2d", [1, 512], F32))
        qrow = E(nc.sbuf_tensor("p2ssq", [1, 512], F32))
        dssq = E(nc.sbuf_tensor("p2dssq", [128, 256], F32))
        stile = E(nc.sbuf_tensor("p2st", [128, 128], F32))
        srow = [E(nc.sbuf_tensor(f"p2sr{i}", [1, 512], F32)) for i in range(2)]
        gch = [E(nc.sbuf_tensor(f"p2g{i}", [128, 512], BF16)) for i in range(2)]
        t1b = E(nc.sbuf_tensor("p2t1", [128, 512], BF16))
        gout = [E(nc.sbuf_tensor(f"p2go{i}", [128, 512], BF16)) for i in range(2)]
        zt = E(nc.sbuf_tensor("p2zt", [128, D], BF16))
        k.cfts = E(nc.sbuf_tensor("cfts", [128, 2176], F32))
        k.tri4s = E(nc.sbuf_tensor("tri4s", [128, 4, 512], BF16))

        SpsH = [[k.ps[0], k.ps[1]], [k.ps[2], k.ps[3]]]
        retP = [k.ps[4], k.ps[5]]
        dP = [k.ps[6], k.ps[7]]

        p = Plan(k)
        G = Guard()

        def gz(g, s=ge.h):
            g.memset(zt[:], 0.0).then_inc(s, 1)
        p.add("gpsimd", gz)
        zc = ge.bump()
        p.tw("gpsimd", (ge, zc))
        nrows = (NE * CAP + 128) // 128
        for i in range(nrows):
            p.dma("gpsimd", k.XSC[i * 128:(i + 1) * 128, :], zt[:])
        k.xsc_zero = p.qdrain("gpsimd")

        cftok = [p.dma("sync", k.cfts[:], k.cft),
                 p.dma("sync", k.tri4s.ap(), k.tri4.rearrange("a p n -> p a n"))]
        for pair in range(P2_PAIRS):
            if pair == 0:
                p.tw("sync", k.p1_done + k.p0_done)
                p.tw("vector", cftok)
            G.use(p, "sync", "qs")
            c_q = p.dma("sync", qs[:], k.QT[pair * 128:(pair + 1) * 128, :])
            G.use(p, "sync", "ks")
            c_k = p.dma("sync", ks[:], k.KT[pair * 128:(pair + 1) * 128, :])
            G.use(p, "sync", "vs")
            c_v = p.dma("sync", vs.ap(),
                        k.VT.rearrange("(a p) v -> p a v", p=128)[:, :,
                                                                 pair * 256:(pair + 1) * 256])

            groups = [(qc, kt) for qc in range(P2_QCMAX) for kt in range(4 * qc + 4)]
            qk_done = {}
            ev_done = {}
            ab_done = {}
            pvd = {}
            epi = {}
            cfbase = pair * 1088
            pair_entry_ve = ve.n

            def emit_pv(g):
                qc, kt = groups[g]
                first = (kt == 0)
                last = (kt == 4 * qc + 3)
                buf = g % 2
                wB = ev_done[g]
                aB = ab_done[g]
                ep = epi.get(qc - 1) if first and qc > 0 else None
                def pv(t, qc=qc, kt=kt, buf=buf, wB=wB, aB=aB, first=first,
                       last=last, ep=ep, s=te.h):
                    t.wait_ge(ve.h, wB)
                    if ep is not None:
                        t.wait_ge(ve.h, ep["ycopy"])
                        t.wait_ge(se.h, ep["square"])
                        t.wait_ge(ve.h, ep["ssqcopy"])
                    t.matmul(retP[0][:, :], vs[:, kt, 0:128], Ssb[0][buf][:],
                             start=first, stop=last)
                    t.matmul(retP[1][:, :], vs[:, kt, 128:256], Ssb[1][buf][:],
                             start=first, stop=last)
                    t.wait_ge(se.h, aB)
                    if P2_DMM:
                        t.matmul(dP[0][0:1, :], k.onescb[:, 0:1], Sab[0][buf][:],
                                 start=first, stop=last)
                        t.matmul(dP[1][0:1, :], k.onescb[:, 0:1], Sab[1][buf][:],
                                 start=first, stop=last).then_inc(s, 1)
                    else:
                        t.matmul(retP[0][:, :], vs[:, kt, 0:128], Sab[0][buf][:],
                                 start=False, stop=False)
                        t.matmul(retP[1][:, :], vs[:, kt, 128:256], Sab[1][buf][:],
                                 start=False, stop=False).then_inc(s, 1)
                p.add("tensor", pv)
                return te.bump()

            def emit_epilogue(qc, ret_done):
                ee = {}
                def yc(v, qc=qc, c=ret_done, s=ve.h):
                    v.wait_ge(te.h, c)
                    v.tensor_copy(yraw[0][:, qc * 512:(qc + 1) * 512],
                                  retP[0][:, :]).then_inc(s, 1)
                    v.tensor_copy(yraw[1][:, qc * 512:(qc + 1) * 512],
                                  retP[1][:, :]).then_inc(s, 1)
                p.add("vector", yc)
                ve.bump(2)
                ee["ycopy"] = ve.n
                def sqf(s_, c=ret_done, cy=ee["ycopy"], s=se.h):
                    s_.wait_ge(te.h, c)
                    s_.wait_ge(ve.h, cy)
                    s_.activation(r2[0][:], retP[0][:, :], AF.Square).then_inc(s, 1)
                    s_.activation(r2[1][:], retP[1][:, :], AF.Square).then_inc(s, 1)
                p.add("scalar", sqf)
                se.bump(2)
                ee["square"] = se.n
                for h in range(2):
                    def dc_(v, h=h, c=ret_done, s=ve.h):
                        v.wait_ge(te.h, c)
                        v.tensor_copy(drow[0:1, :], dP[h][0:1, :]).then_inc(s, 1)
                    p.add("vector", dc_)
                    c_dc = ve.bump()
                    p.tw("scalar", (ve, c_dc))
                    stc = p.dma("scalar", k.SCRD[h:h + 1, qc * 512:(qc + 1) * 512],
                                drow[0:1, :])
                    p.tw("vector", stc)
                ee["dcopy"] = ve.n
                def ssqm(t, c1=ee["square"], c2=ee["dcopy"], s=te.h):
                    t.wait_ge(se.h, c1)
                    t.wait_ge(ve.h, c2)
                    t.matmul(dP[0][0:1, :], k.onescb[:, 0:1], r2[0][:],
                             start=True, stop=True)
                    t.matmul(dP[1][0:1, :], k.onescb[:, 0:1], r2[1][:],
                             start=True, stop=True).then_inc(s, 1)
                p.add("tensor", ssqm)
                c_ssq = te.bump()
                for h in range(2):
                    def qc_(v, h=h, c=c_ssq, s=ve.h):
                        v.wait_ge(te.h, c)
                        v.tensor_copy(qrow[0:1, :], dP[h][0:1, :]).then_inc(s, 1)
                    p.add("vector", qc_)
                    c_qc = ve.bump()
                    p.tw("scalar", (ve, c_qc))
                    stc = p.dma("scalar", k.SCRD[2 + h:3 + h, qc * 512:(qc + 1) * 512],
                                qrow[0:1, :])
                    p.tw("vector", stc)
                ee["ssqcopy"] = ve.n
                epi[qc] = ee

            for g, (qc, kt) in enumerate(groups if P2_CORE else []):
                buf = g % 2
                diag = kt >= 4 * qc
                dk = kt - 4 * qc
                if g == 0:
                    p.tw("tensor", [c_q, c_k, c_v])
                ev2 = ev_done.get(g - 2, 0)
                abq2 = ab_done.get(g - 2, 0)
                def qk(t, qc=qc, kt=kt, buf=buf, g=g, pe=pair_entry_ve,
                       ev2=ev2, abq2=abq2, s=te.h):
                    if g == 0:
                        t.wait_ge(ve.h, pe)
                    if g >= 2:
                        t.wait_ge(ve.h, ev2)
                        t.wait_ge(se.h, abq2)
                    t.matmul(SpsH[0][buf][:, :], ks[0:64, kt * 128:(kt + 1) * 128],
                             qs[0:64, qc * 512:(qc + 1) * 512], start=True, stop=True,
                             tile_position=(0, 0))
                    t.matmul(SpsH[1][buf][:, :], ks[64:128, kt * 128:(kt + 1) * 128],
                             qs[64:128, qc * 512:(qc + 1) * 512], start=True, stop=True,
                             tile_position=(64, 0)).then_inc(s, 1)
                p.add("tensor", qk)
                qk_done[g] = te.bump()
                if g >= 1:
                    rd = emit_pv(g - 1)
                    pvd[g - 1] = rd
                    pqc, pkt = groups[g - 1]
                    if pkt == 4 * pqc + 3 and P2_EPI:
                        emit_epilogue(pqc, rd)
                for h in range(2):
                    col = cfbase + h * 544 + sum(4 * q + 4 for q in range(qc)) + kt
                    ab2 = ab_done.get(g - 2, 0)
                    def evh(v, h=h, buf=buf, col=col, diag=diag, dk=dk,
                            c=qk_done[g], ab2=ab2, s=ve.h):
                        v.wait_ge(te.h, c)
                        if ab2 > 0:
                            v.wait_ge(se.h, ab2)
                        cfap = k.cfts[:, col:col + 1] if P2_CF else 1.0
                        if diag:
                            v.tensor_scalar(Srw[h][:], SpsH[h][buf][:, :],
                                            cfap, None, ALU.mult)
                            v.tensor_tensor(Ssb[h][buf][:], Srw[h][:],
                                            k.tri4s[:, dk, :], ALU.mult).then_inc(s, 1)
                        else:
                            v.tensor_scalar(Ssb[h][buf][:], SpsH[h][buf][:, :],
                                            cfap, None,
                                            ALU.mult).then_inc(s, 1)
                    p.add("vector", evh)
                    ve.bump()
                ev_done[g] = ve.n
                for h in range(2):
                    pv2 = pvd.get(g - 2, 0)
                    def abh(s_, h=h, buf=buf, c=ev_done[g], pv2=pv2, s=se.h):
                        s_.wait_ge(ve.h, c)
                        if pv2 > 0:
                            s_.wait_ge(te.h, pv2)
                        s_.activation(Sab[h][buf][:], Ssb[h][buf][:],
                                      AF.Abs).then_inc(s, 1)
                    p.add("scalar", abh)
                    se.bump()
                ab_done[g] = se.n
            if P2_CORE:
                rd = emit_pv(len(groups) - 1)
                pvd[len(groups) - 1] = rd
                if P2_EPI:
                    emit_epilogue(P2_QCMAX - 1, rd)

            G.consumed("qs", (te, te.n))
            G.consumed("ks", (te, te.n))
            G.consumed("vs", (te, te.n))

            if not P2_GATING:
                continue
            sd_drain = p.qdrain("scalar")
            p.tw("sync", sd_drain)
            G.use(p, "sync", "dssq")
            c_ds = p.dma("sync", dssq.ap().rearrange("p (r n) -> p r n", r=4),
                         k.SCRD.rearrange("r (p n) -> p r n", p=128))
            dpart = dssq[:, 0:128]
            spart = dssq[:, 128:256]
            p.tw("vector", c_ds)
            def sm1(v, base=ve.n, s=ve.h):
                v.tensor_scalar(dpart, dpart, 1.0, 50000.0, ALU.max,
                                ALU.min).then_inc(s, 1)
                v.tensor_scalar(spart, spart, 1.0 / VD, None,
                                ALU.mult).then_inc(s, 1)
                v.wait_ge(s, base + 2)
                v.reciprocal(stile[:], dpart).then_inc(s, 1)
                v.wait_ge(s, base + 3)
                v.tensor_tensor(spart, spart, stile[:], ALU.mult).then_inc(s, 1)
                v.wait_ge(s, base + 4)
                v.scalar_tensor_tensor(spart, spart, 1.0, stile[:], ALU.mult,
                                       ALU.mult).then_inc(s, 1)
                v.wait_ge(s, base + 5)
                v.tensor_scalar(spart, spart, EPS_ATTN, None,
                                ALU.add).then_inc(s, 1)
            p.add("vector", sm1)
            ve.bump(6)
            c_sm1 = ve.n
            def sm2(s_, c=c_sm1, s=se.h):
                s_.wait_ge(ve.h, c)
                s_.activation(dpart, spart, AF.Sqrt).then_inc(s, 1)
            p.add("scalar", sm2)
            c_sm2 = se.bump()
            def sm3(v, c=c_sm2, cv=c_sm1, base=ve.n, s=ve.h):
                v.wait_ge(se.h, c)
                v.wait_ge(s, cv)
                v.reciprocal(spart, dpart).then_inc(s, 1)
                v.wait_ge(s, base + 1)
                v.tensor_tensor(stile[:], stile[:], spart, ALU.mult).then_inc(s, 1)
            p.add("vector", sm3)
            ve.bump(2)
            c_sm3 = ve.n
            p.tw("sync", (ve, c_sm3))
            c_ss = p.dma("sync", k.SCRS.rearrange("r (p n) -> p r n", p=128),
                         stile.ap().rearrange("p (r n) -> p r n", r=2))
            G.consumed("dssq", (ve, c_sm3))

            for h in range(2):
                L = pair * 2 + h
                for qc in range(P2_QCMAX):
                    pvb = qc % 2
                    G.use(p, "scalar", ("srow", pvb))
                    if qc == 0:
                        p.tw("scalar", c_ss)
                    c_sr = p.dma("scalar", srow[pvb][:],
                                 k.SCRS[h:h + 1, qc * 512:(qc + 1) * 512])
                    G.use(p, "scalar", ("gch", pvb))
                    c_g = p.dma("scalar", gch[pvb][:],
                                k.GT[L * 128:(L + 1) * 128, qc * 512:(qc + 1) * 512])
                    G.use(p, "tensor", "k1bank")
                    p.tw("tensor", [c_sr, c_ss, k.ones_ready])
                    def k1(t, pvb=pvb, s=te.h):
                        t.matmul(SpsH[0][0][:, :], k.onesrf[0:1, :], srow[pvb][0:1, :],
                                 start=True, stop=True).then_inc(s, 1)
                    p.add("tensor", k1)
                    c_k1 = te.bump()
                    G.consumed(("srow", pvb), (te, c_k1))
                    G.use(p, "vector", ("gout", pvb))
                    p.tw("vector", c_g)
                    yr = epi[qc]["ycopy"]
                    def gate(v, h=h, qc=qc, pvb=pvb, c1=c_k1, yr=yr, s=ve.h):
                        v.wait_ge(te.h, c1)
                        v.wait_ge(s, yr)
                        v.tensor_tensor(t1b[:], yraw[h][:, qc * 512:(qc + 1) * 512],
                                        SpsH[0][0][:, :], ALU.mult)
                        v.tensor_tensor(gout[pvb][:], t1b[:], gch[pvb][:],
                                        ALU.mult).then_inc(s, 1)
                    p.add("vector", gate)
                    c_gt = ve.bump()
                    G.consumed("k1bank", (ve, c_gt))
                    G.consumed(("gch", pvb), (ve, c_gt))
                    p.tw("sync", (ve, c_gt))
                    stc = p.dma("sync",
                                k.CCI[L * 128:(L + 1) * 128, qc * 512:(qc + 1) * 512],
                                gout[pvb][:])
                    G.consumed(("gout", pvb), stc)
            if pair == 0:
                k.cci_p0 = p.qdrain("sync")
                p.tw("gpsimd", k.cci_p0)
                def ccf1(g, s=k.cc.h):
                    g.collective_compute(
                        "AllGather", ALU.bypass,
                        replica_groups=[[0, 1, 2, 3], [4, 5, 6, 7]],
                        ins=[k.CCI[0:256, :].opt()],
                        outs=[k.CCO[0:1024, :].opt()],
                    ).then_inc(s)
                p.add("gpsimd", ccf1)
                k.cc.bump()
        k.cci_done = p.qdrain("sync")
        p.tw("gpsimd", k.cci_done)
        def ccf2(g, s=k.cc.h):
            g.collective_compute(
                "AllGather", ALU.bypass,
                replica_groups=[[0, 1, 2, 3], [4, 5, 6, 7]],
                ins=[k.CCI[256:512, :].opt()],
                outs=[k.CCO[1024:2048, :].opt()],
            ).then_inc(s)
        p.add("gpsimd", ccf2)
        k.cc.bump()
        p.emit()


def phase3(k):
    pass


def phase4(k):
    nc = k.nc
    te, ve, se, ge = k.te, k.ve, k.se, k.ge
    with ExitStack() as ectx:
        E = ectx.enter_context
        ccl = [E(nc.sbuf_tensor(f"p4cc{i}", [128, 2048], BF16)) for i in range(16)]
        wob = E(nc.sbuf_tensor("p4wo", [128, 16, 1024], BF16))
        borow = E(nc.sbuf_tensor("p4bo", [1, 1024], BF16))
        xot = [E(nc.sbuf_tensor(f"p4xo{i}", [128, D], F32)) for i in range(2)]
        ln2f = E(nc.sbuf_tensor("p4ln2", [128, D], F32))
        x2t = [E(nc.sbuf_tensor(f"p4x2{i}", [128, D], F32)) for i in range(2)]
        sqd4 = E(nc.sbuf_tensor("p4sqd", [128, D], BF16))
        colsA = E(nc.sbuf_tensor("p4colsA", [128, 8], F32))
        colsB = E(nc.sbuf_tensor("p4colsB", [128, 8], F32))
        xn2t = E(nc.sbuf_tensor("p4xn2", [128, D], BF16))

        p = Plan(k)
        G = Guard()

        for vt in range(16):
            if vt == 0:
                p.tw("gpsimd", (k.cc, 1))
            if vt == 8:
                p.tw("gpsimd", (k.cc, 2))
            half = vt // 8
            sem, cnt = p.slot("gpsimd")
            def gth(g, vt=vt, s=sem.h):
                g.indirect_dma_start(
                    out=ccl[vt][:], out_offset=None,
                    in_=k.CCO.rearrange("v (q n) -> (v q) n", q=4),
                    in_offset=IndirectOffsetOnAxis(k.ccidxs[:, vt:vt + 1], 0),
                ).then_inc(s, 16)
            p.add("gpsimd", gth)
        cc_toks = p.qdrain("gpsimd")
        c_wo = p.dma("gpsimd", wob.ap(), k.wo.rearrange("(a p) d -> p a d", p=128))
        c_bo = p.dma("gpsimd", borow[:], k.bor)

        for tc in range(16 if P4_A else 0):
            pb = tc % 2
            G.use(p, "sync", ("xot", pb))
            c_xo = p.dma("sync", xot[pb][:], k.xown[tc * 128:(tc + 1) * 128, :])
            G.use(p, "tensor", "wops")
            if tc == 0:
                p.tw("tensor", cc_toks + [c_wo, c_bo, k.ones_ready])
            def mmo(t, tc=tc, s=te.h):
                for half in range(2):
                    for vt in range(16):
                        hd = 4 * ((vt % 8) // 2) + 2 * (vt // 8) + (vt % 8) % 2
                        t.matmul(k.ps[half][:, :], ccl[vt][:, tc * 128:(tc + 1) * 128],
                                 wob[:, hd, half * 512:(half + 1) * 512],
                                 start=(vt == 0), stop=False)
                    t.matmul(k.ps[half][:, :], k.onesr[0:1, :],
                             borow[0:1, half * 512:(half + 1) * 512],
                             start=False, stop=True).then_inc(s, 1)
            p.add("tensor", mmo)
            te.bump()
            c_mm = te.bump()
            s1 = colsA[:, 0:1]; sqc = colsA[:, 1:2]
            rstd = colsA[:, 2:3]; nmr = colsA[:, 3:4]
            tb = colsA[:, 4:5]; td = colsA[:, 5:6]
            G.use(p, "vector", "colsA")
            def st1(v, c=c_mm, s1=s1, tb=tb, base=ve.n, s=ve.h):
                v.wait_ge(te.h, c)
                v.tensor_reduce(s1, k.ps[0][:, :], AX.X, ALU.add).then_inc(s, 1)
                v.tensor_reduce(tb, k.ps[1][:, :], AX.X, ALU.add).then_inc(s, 1)
                v.wait_ge(s, base + 2)
                v.tensor_tensor(s1, s1, tb, ALU.add).then_inc(s, 1)
            p.add("vector", st1)
            ve.bump(2)
            c_s1 = ve.bump()
            def st2(s_, c=c_mm, c2=c_s1, sqc=sqc, td=td, s=se.h):
                s_.wait_ge(te.h, c)
                s_.wait_ge(ve.h, c2)
                s_.activation(sqd4[:, 0:512], k.ps[0][:, :], AF.Square,
                              accum_out=sqc).then_inc(s, 1)
                s_.activation(sqd4[:, 512:1024], k.ps[1][:, :], AF.Square,
                              accum_out=td).then_inc(s, 1)
            p.add("scalar", st2)
            se.bump()
            c_s2 = se.bump()
            def st3(v, c=c_s2, s1=s1, sqc=sqc, rstd=rstd, nmr=nmr, tb=tb, td=td,
                    base=ve.n, s=ve.h):
                v.wait_ge(se.h, c)
                v.tensor_tensor(sqc, sqc, td, ALU.add).then_inc(s, 1)
                v.tensor_scalar(nmr, s1, 1.0 / D, None, ALU.mult).then_inc(s, 1)
                v.scalar_tensor_tensor(tb, s1, 1.0 / (D * D), s1, ALU.mult,
                                       ALU.mult).then_inc(s, 1)
                v.wait_ge(s, base + 3)
                v.tensor_scalar(rstd, sqc, 1.0 / D, EPS_LN, ALU.mult,
                                ALU.add).then_inc(s, 1)
                v.wait_ge(s, base + 4)
                v.scalar_tensor_tensor(rstd, tb, -1.0, rstd, ALU.mult,
                                       ALU.add).then_inc(s, 1)
            p.add("vector", st3)
            ve.bump(4)
            c_s3 = ve.bump()
            def st4(s_, c=c_s3, rstd=rstd, tb=tb, s=se.h):
                s_.wait_ge(ve.h, c)
                s_.activation(tb, rstd, AF.Sqrt).then_inc(s, 1)
            p.add("scalar", st4)
            c_s4 = se.bump()
            def st5(v, c=c_s4, rstd=rstd, nmr=nmr, tb=tb, base=ve.n, s=ve.h):
                v.wait_ge(se.h, c)
                v.reciprocal(rstd, tb).then_inc(s, 1)
                v.wait_ge(s, base + 1)
                v.scalar_tensor_tensor(nmr, nmr, -1.0, rstd, ALU.mult,
                                       ALU.mult).then_inc(s, 1)
            p.add("vector", st5)
            ve.bump()
            c_s5 = ve.bump()
            def st6(s_, c=c_s5, rstd=rstd, nmr=nmr, s=se.h):
                s_.wait_ge(ve.h, c)
                s_.activation(ln2f[:, 0:512], k.ps[0][:, :], AF.Identity,
                              bias=nmr, scale=rstd).then_inc(s, 1)
                s_.activation(ln2f[:, 512:1024], k.ps[1][:, :], AF.Identity,
                              bias=nmr, scale=rstd).then_inc(s, 1)
            p.add("scalar", st6)
            se.bump()
            c_s6 = se.bump()
            G.consumed("colsA", (se, c_s6))
            G.use(p, "vector", ("x2t", pb))
            p.tw("vector", c_xo)
            def st7(v, pb=pb, c1=c_s6, s=ve.h):
                v.wait_ge(se.h, c1)
                v.tensor_tensor(x2t[pb][:], xot[pb][:], ln2f[:], ALU.add).then_inc(s, 1)
            p.add("vector", st7)
            c_x2 = ve.bump()
            G.consumed(("xot", pb), (ve, c_x2))
            p.tw("sync", (ve, c_x2))
            p.dma("sync", k.X2S[tc * 128:(tc + 1) * 128, :], x2t[pb][:])
            G.use(p, "vector", "colsB")
            p.tw("scalar", (ve, c_x2))
            rstd3, nmr3, c_c3 = ln_cols(k, p, x2t[pb][:], colsB[:], 0, sqd4[:])
            G.use(p, "scalar", "xn2t")
            def st8(s_, a=x2t[pb][:], o=xn2t[:], sc=rstd3, bi=nmr3, c=c_c3, s=se.h):
                s_.wait_ge(ve.h, c)
                s_.activation(o, a, AF.Identity, bias=bi, scale=sc).then_inc(s, 1)
            p.add("scalar", st8)
            c_xn2 = se.bump()
            G.consumed("colsB", (se, c_xn2))
            G.consumed(("x2t", pb), (se, c_xn2))
            p.tw("sync", (se, c_xn2))
            stc = p.dma("sync", k.XT2[tc * 128:(tc + 1) * 128, :], xn2t[:])
            G.consumed("xn2t", stc)
            G.consumed("wops", (se, c_s6))
            p.tw("tensor", (ve, c_s1))

        k.p4a_done = p.qdrain("sync")
        k.xt2_store = None
        p.emit()

    if P4_B:
        phase4b(k)
        if UPTO >= 5:
            phase4c(k)
            phase4d(k)
        else:
            k.e4b.close()


def phase4b(k):
    nc = k.nc
    te, ve, se, ge = k.te, k.ve, k.se, k.ge
    ectx = ExitStack()
    k.e4b = ectx
    E = ectx.enter_context
    xt2T = E(nc.sbuf_tensor("p4xt2T", [128, 8, 2048], BF16))
    k.xt2T = xt2T
    slotc = k.slotc

    with ExitStack() as sctx:
        Es = sctx.enter_context
        exall = Es(nc.sbuf_tensor("p4ex", [128, 128], F32))
        lall = Es(nc.sbuf_tensor("p4l", [128, 128], F32))
        pall = Es(nc.sbuf_tensor("p4p", [128, 128], F32))
        p2all = Es(nc.sbuf_tensor("p4p2", [128, 128], F32))
        eqall = Es(nc.sbuf_tensor("p4eq", [128, 16, 8], F32))
        t8all = Es(nc.sbuf_tensor("p4t8", [128, 16, 8], F32))
        colsr = Es(nc.sbuf_tensor("p4cr", [128, 16, 8], F32))
        iota6 = Es(nc.sbuf_tensor("p4i6", [128, 8], F32))
        iota0 = Es(nc.sbuf_tensor("p4i0", [128, 8], F32))
        eio = Es(nc.sbuf_tensor("p4ei", [8, 1], F32))
        ones8 = Es(nc.sbuf_tensor("p4o8", [8, 2048], F16))
        idxr8 = [Es(nc.sbuf_tensor(f"p4ir{i}", [8, 2048], F16)) for i in range(2)]
        selT = Es(nc.sbuf_tensor("p4sT", [8, 2048], F16))
        cntT = Es(nc.sbuf_tensor("p4cT", [8, 2048], F16))
        cumT = Es(nc.sbuf_tensor("p4cmT", [8, 2048], F16))
        cold16 = Es(nc.sbuf_tensor("p4c16", [128, 2, 16], F16))
        rrowS = Es(nc.sbuf_tensor("p4rr", [1, 1024], F32))
        yzh = Es(nc.sbuf_tensor("p4yzh", [128, 512], F32))
        rnkc = Es(nc.sbuf_tensor("p4rk", [128, 16], F32))
        tm16a = Es(nc.sbuf_tensor("p4ta", [128, 16], F32))
        tm16b = Es(nc.sbuf_tensor("p4tb", [128, 16], F32))
        tm16c = Es(nc.sbuf_tensor("p4tc", [128, 16], F32))
        dmpc = Es(nc.sbuf_tensor("p4dm", [128, 1], F32))
        tokc = Es(nc.sbuf_tensor("p4tk", [128, 16], F32))
        trsc = Es(nc.sbuf_tensor("p4tr", [128, 1], F32))
        slott = Es(nc.sbuf_tensor("p4st2", [128, 2, 16], U32))
        xn2l = [Es(nc.sbuf_tensor(f"p4xl{i}", [128, D], BF16)) for i in range(2)]
        xs0 = [Es(nc.sbuf_tensor(f"p4xs{i}", [128, D], BF16)) for i in range(2)]
        sw1s = Es(nc.sbuf_tensor("p4sw1", [128, 8, 2048], BF16))
        sw2s = Es(nc.sbuf_tensor("p4sw2", [128, 8, 2048], BF16))
        sw3s = Es(nc.sbuf_tensor("p4sw3", [128, 16, D], BF16))
        zs1 = Es(nc.sbuf_tensor("p4z1", [128, 512], BF16))
        zsl = Es(nc.sbuf_tensor("p4zs", [128, 16, 512], BF16))

        p = Plan(k)

        def guard():
            avn = ve.n
            def gf(g, avn=avn, s=ge.h):
                g.wait_ge(ve.h, avn)
                g.nop().then_inc(s, 1)
            p.add("gpsimd", gf)
            c = ge.bump()
            p.tw("vector", (ge, c))

        t1 = p.dma("sync", iota6[:], k.iot6)
        t2 = p.dma("sync", iota0[:], k.iot0)
        t3 = p.dma("sync", eio[:], k.eioc)
        t4 = p.dma("sync", dmpc[:], k.dmpp)
        t5 = p.dma("sync", tokc[:], k.tokf)
        t6 = p.dma("sync", trsc[:], k.trsf)
        iot_toks = [t1, t2, t3, t4, t5, t6]
        p.tw("gpsimd", (te, te.n))
        c_sw1 = p.dma("gpsimd", sw1s.ap(), k.sw1T.rearrange("(c p) i -> p c i", p=128))
        c_sw2 = p.dma("gpsimd", sw2s.ap(), k.sw2T.rearrange("(c p) i -> p c i", p=128))
        c_sw3 = p.dma("gpsimd", sw3s.ap(), k.sw3T.rearrange("(c p) d -> p c d", p=128))
        def vo8(v, s=ve.h):
            v.memset(ones8[:], 1.0).then_inc(s, 1)
        p.add("vector", vo8)
        c_o8 = ve.bump()

        p.tw("sync", k.p4a_done)
        for dc in range(8):
            p.dma("sync", xt2T[:, dc, :], k.XT2[:, dc * 128:(dc + 1) * 128],
                  transpose=True)
        k.xt2T_toks = p.qdrain("sync")

        lg = [k.ps[2], k.ps[3]]
        exp_done = {}
        p.tw("tensor", k.xt2T_toks + [k.gw_ready])
        for t16 in range(16):
            ed2 = exp_done.get(t16 - 2, 0)
            def mml(t, t16=t16, ed2=ed2, s=te.h):
                if t16 >= 2:
                    t.wait_ge(se.h, ed2)
                for dc in range(8):
                    ins = t.matmul(lg[t16 % 2][:, 0:8],
                                   xt2T[:, dc, t16 * 128:(t16 + 1) * 128],
                                   k.gwTs[:, dc, :], start=(dc == 0), stop=(dc == 7))
                ins.then_inc(s, 1)
            p.add("tensor", mml)
            c_mm = te.bump()
            def ex(s_, t16=t16, c=c_mm, s=se.h):
                s_.wait_ge(te.h, c)
                s_.activation(exall[:, 8 * t16:8 * t16 + 8], lg[t16 % 2][:, 0:8],
                              AF.Exp).then_inc(s, 1)
                s_.activation(lall[:, 8 * t16:8 * t16 + 8], lg[t16 % 2][:, 0:8],
                              AF.Identity).then_inc(s, 1)
            p.add("scalar", ex)
            se.bump()
            exp_done[t16] = se.bump()

        zp = [[k.ps[0], k.ps[1]], [k.ps[2], k.ps[3]]]
        yp = [k.ps[4], k.ps[5]]
        exd15 = exp_done[15]
        sh = dict(z_done={}, sz_done={}, prev_yz=None, gi=0, armed=False)
        yzs_toks = []

        def emit_mmy(u, col):
            t16 = u * 4 + col
            def mmy(t, u=u, col=col, c1=sh["z_done"].get(u * 16 + 15, 0),
                    prv=sh["prev_yz"], s=te.h):
                t.wait_ge(ve.h, c1)
                for half in range(2):
                    for ic in range(16):
                        ins = t.matmul(yp[half][:, :],
                                       zsl[:, ic, col * 128:(col + 1) * 128],
                                       sw3s[:, ic, half * 512:(half + 1) * 512],
                                       start=(ic == 0), stop=(ic == 15))
                ins.then_inc(s, 1)
            if sh["prev_yz"]:
                p.tw("tensor", sh["prev_yz"])
            p.add("tensor", mmy)
            c_my = te.bump()
            stc = []
            for half in range(2):
                def vyh(v, c=c_my, half=half,
                        pv=(stc[-1] if stc else sh.get("prev_yzst")),
                        s=ve.h):
                    v.wait_ge(te.h, c)
                    if pv is not None:
                        v.wait_ge(pv[0].h, pv[1])
                    v.tensor_copy(yzh[:], yp[half][:, :]).then_inc(s, 1)
                p.add("vector", vyh)
                c_vy = ve.bump()
                p.tw("sync", (ve, c_vy))
                stc.append(p.dma("sync",
                                 k.YZS[t16 * 128:(t16 + 1) * 128,
                                       half * 512:(half + 1) * 512], yzh[:]))
            sh["prev_yz"] = [(ve, ve.n)]
            sh["prev_yzst"] = stc[1]
            yzs_toks.extend(stc)

        def emit_groups(n):
            for _ in range(n):
                gi = sh["gi"]
                if gi >= 64 or UPTO < 5:
                    return
                u, it = gi // 16, gi % 16
                pz = gi % 2
                if not sh["armed"]:
                    p.tw("tensor", [c_sw1, c_sw2] + k.xt2T_toks)
                    sh["armed"] = True
                zd2 = sh["z_done"].get(gi - 2, 0)
                sd2 = sh["sz_done"].get(gi - 2, 0)
                def mmz(t, u=u, it=it, pz=pz, gi=gi, zd2=zd2, sd2=sd2, s=te.h):
                    if gi == 1:
                        t.wait_ge(se.h, exd15)
                    if zd2 > 0:
                        t.wait_ge(ve.h, zd2)
                    if sd2 > 0:
                        t.wait_ge(se.h, sd2)
                    for dc in range(8):
                        t.matmul(zp[pz][0][:, :], sw1s[:, dc, it * 128:(it + 1) * 128],
                                 xt2T[:, dc, u * 512:(u + 1) * 512],
                                 start=(dc == 0), stop=(dc == 7))
                    for dc in range(8):
                        ins = t.matmul(zp[pz][1][:, :], sw2s[:, dc, it * 128:(it + 1) * 128],
                                       xt2T[:, dc, u * 512:(u + 1) * 512],
                                       start=(dc == 0), stop=(dc == 7))
                    ins.then_inc(s, 1)
                p.add("tensor", mmz)
                c_mz = te.bump()
                zprev = sh["z_done"].get(gi - 1, 0)
                def sz(s_, pz=pz, c=c_mz, zprev=zprev, s=se.h):
                    s_.wait_ge(te.h, c)
                    if zprev > 0:
                        s_.wait_ge(ve.h, zprev)
                    s_.activation(zs1[:], zp[pz][0][:, :], AF.Silu).then_inc(s, 1)
                p.add("scalar", sz)
                c_sz = se.bump()
                sh["sz_done"][gi] = c_sz
                def vz(v, it=it, pz=pz, c=c_sz, s=ve.h):
                    v.wait_ge(se.h, c)
                    v.tensor_tensor(zsl[:, it, :], zp[pz][1][:, :], zs1[:],
                                    ALU.mult).then_inc(s, 1)
                p.add("vector", vz)
                sh["z_done"][gi] = ve.bump()
                sh["gi"] = gi + 1
                if it == 15:
                    p.tw("tensor", c_sw3)
                    for col in range(4):
                        emit_mmy(u, col)

        def vsum(v, c=exp_done[15], s=ve.h):
            v.wait_ge(se.h, c)
            for t16 in range(16):
                v.tensor_reduce(colsr[:, t16, 4:5], exall[:, 8 * t16:8 * t16 + 8],
                                AX.X, ALU.add).then_inc(s, 1)
        p.add("vector", vsum)
        ve.bump(16)
        def vrcp(v, s=ve.h):
            for t16 in range(16):
                v.reciprocal(colsr[:, t16, 5:6], colsr[:, t16, 4:5]).then_inc(s, 1)
        p.add("vector", vrcp)
        ve.bump(16)
        c_rcp = ve.n
        def pnorm(s_, c=c_rcp, s=se.h):
            s_.wait_ge(ve.h, c)
            for t16 in range(16):
                s_.activation(pall[:, 8 * t16:8 * t16 + 8],
                              exall[:, 8 * t16:8 * t16 + 8],
                              AF.Identity, scale=colsr[:, t16, 5:6]).then_inc(s, 1)
        p.add("scalar", pnorm)
        se.bump(16)
        c_pn = se.n
        def vm0(v, c=c_pn, s=ve.h):
            v.wait_ge(se.h, c)
            for t16 in range(16):
                v.tensor_reduce(colsr[:, t16, 0:1], lall[:, 8 * t16:8 * t16 + 8],
                                AX.X, ALU.max).then_inc(s, 1)
            for t16 in range(16):
                v.tensor_reduce(colsr[:, t16, 6:7], pall[:, 8 * t16:8 * t16 + 8],
                                AX.X, ALU.max).then_inc(s, 1)
        p.add("vector", vm0)
        ve.bump(32)
        guard()
        emit_groups(2)
        p.tw("vector", iot_toks)
        def vtop2a(v, s=ve.h):
            for t16 in range(16):
                v.tensor_scalar(eqall[:, t16, :], lall[:, 8 * t16:8 * t16 + 8],
                                colsr[:, t16, 0:1], None, ALU.is_equal)
            for t16 in range(16):
                v.scalar_tensor_tensor(t8all[:, t16, :], eqall[:, t16, :],
                                       -1000000.0, iota6[:], ALU.mult, ALU.add)
            for t16 in range(16):
                v.tensor_reduce(colsr[:, t16, 1:2], t8all[:, t16, :], AX.X,
                                ALU.min).then_inc(s, 1)
        p.add("vector", vtop2a)
        ve.bump(16)
        guard()
        emit_groups(2)
        def vtop2b(v, s=ve.h):
            for t16 in range(16):
                v.tensor_scalar(eqall[:, t16, :], iota0[:],
                                colsr[:, t16, 1:2], None, ALU.is_equal)
            for t16 in range(16):
                v.scalar_tensor_tensor(p2all[:, 8 * t16:8 * t16 + 8],
                                       eqall[:, t16, :], -1e30,
                                       lall[:, 8 * t16:8 * t16 + 8],
                                       ALU.mult, ALU.add)
            for t16 in range(16):
                v.scalar_tensor_tensor(exall[:, 8 * t16:8 * t16 + 8],
                                       eqall[:, t16, :], -1e30,
                                       pall[:, 8 * t16:8 * t16 + 8],
                                       ALU.mult, ALU.add)
            for t16 in range(16):
                v.tensor_reduce(colsr[:, t16, 2:3], p2all[:, 8 * t16:8 * t16 + 8],
                                AX.X, ALU.max).then_inc(s, 1)
            for t16 in range(16):
                v.tensor_reduce(colsr[:, t16, 7:8], exall[:, 8 * t16:8 * t16 + 8],
                                AX.X, ALU.max).then_inc(s, 1)
        p.add("vector", vtop2b)
        ve.bump(32)
        guard()
        emit_groups(2)
        def vtop2c(v, s=ve.h):
            for t16 in range(16):
                v.tensor_scalar(eqall[:, t16, :], p2all[:, 8 * t16:8 * t16 + 8],
                                colsr[:, t16, 2:3], None, ALU.is_equal)
            for t16 in range(16):
                v.scalar_tensor_tensor(t8all[:, t16, :], eqall[:, t16, :],
                                       -1000000.0, iota6[:], ALU.mult, ALU.add)
            for t16 in range(16):
                v.tensor_reduce(colsr[:, t16, 3:4], t8all[:, t16, :], AX.X,
                                ALU.min).then_inc(s, 1)
        p.add("vector", vtop2c)
        ve.bump(16)
        c_top2 = ve.n
        emit_groups(2)

        def vc16(v, s=ve.h):
            v.tensor_copy(cold16[:, 0, :],
                          colsr[:, :, 1:2].rearrange("p t a -> p (t a)")).then_inc(s, 1)
            v.tensor_copy(cold16[:, 1, :],
                          colsr[:, :, 3:4].rearrange("p t a -> p (t a)")).then_inc(s, 1)
        p.add("vector", vc16)
        ve.bump()
        c_c16 = ve.bump()
        p.tw("sync", (ve, c_c16))
        for sl in range(2):
            for t16 in range(16):
                p.dma("sync",
                      k.RID[sl:sl + 1, t16 * 128:(t16 + 1) * 128].rearrange("a n -> n a"),
                      cold16[:, sl, t16:t16 + 1])
        rid_toks = p.qdrain("sync")
        p.tw("scalar", rid_toks)
        for sl in range(2):
            for pp in range(8):
                p.dma("scalar", idxr8[sl][pp:pp + 1, :], k.RID[sl:sl + 1, :])
        idxr_toks = p.qdrain("scalar")
        emit_groups(2)

        p.tw("vector", idxr_toks)
        def vcnt(v, base=ve.n, s=ve.h):
            v.tensor_scalar(selT[:], idxr8[0][:], eio[:, 0:1], None,
                            ALU.is_equal).then_inc(s, 1)
            v.tensor_scalar(cntT[:], idxr8[1][:], eio[:, 0:1], None,
                            ALU.is_equal).then_inc(s, 1)
            v.wait_ge(s, base + 2)
            v.tensor_tensor(cntT[:], cntT[:], selT[:], ALU.add).then_inc(s, 1)
            v.wait_ge(s, base + 3)
            v.tensor_tensor_scan(cumT[:], ones8[:], cntT[:], 0.0, ALU.mult,
                                 ALU.add).then_inc(s, 1)
            v.wait_ge(s, base + 4)
            v.tensor_tensor(cumT[:], cumT[:], cntT[:], ALU.subtract).then_inc(s, 1)
        p.add("vector", vcnt)
        ve.bump(5)
        emit_groups(27)
        slot_ready = []
        sltk_toks = []
        prev_rmm = 0
        for sl in range(2):
            def vsel(v, sl=sl, pr=prev_rmm, base=ve.n, s=ve.h):
                if pr > 0:
                    v.wait_ge(te.h, pr)
                v.tensor_scalar(selT[:], idxr8[sl][:], eio[:, 0:1], None,
                                ALU.is_equal).then_inc(s, 1)
                v.wait_ge(s, base + 1)
                v.tensor_tensor(selT[:], selT[:], cumT[:], ALU.mult).then_inc(s, 1)
            p.add("vector", vsel)
            ve.bump()
            c_sel = ve.bump()
            p.tw("tensor", [(ve, c_sel), k.ones_ready])
            def rmmA(t, s=te.h):
                for q4 in range(2):
                    t.matmul(k.ps[6 + q4][0:1, :], k.onescfh[:, 0:1],
                             selT[:, q4 * 512:(q4 + 1) * 512],
                             start=True, stop=True).then_inc(s, 1)
            p.add("tensor", rmmA)
            c_rmmA = te.bump(2)
            def vrkA(v, c=c_rmmA, s=ve.h):
                v.wait_ge(te.h, c)
                v.tensor_copy(rrowS[0:1, 0:512], k.ps[6][0:1, :]).then_inc(s, 1)
                v.tensor_copy(rrowS[0:1, 512:1024], k.ps[7][0:1, :]).then_inc(s, 1)
            p.add("vector", vrkA)
            ve.bump()
            c_rkA = ve.bump()
            p.tw("sync", (ve, c_rkA))
            rA = [p.dma("sync", k.RNKD[sl:sl + 1, 0:1024], rrowS[0:1, :])]
            p.tw("tensor", (ve, c_rkA))
            def rmmB(t, s=te.h):
                for q4 in range(2):
                    t.matmul(k.ps[6 + q4][0:1, :], k.onescfh[:, 0:1],
                             selT[:, (2 + q4) * 512:(3 + q4) * 512],
                             start=True, stop=True).then_inc(s, 1)
            p.add("tensor", rmmB)
            c_rmm = te.bump(2)
            prev_rmm = c_rmm
            p.tw("vector", rA)
            def vrkB(v, c=c_rmm, s=ve.h):
                v.wait_ge(te.h, c)
                v.tensor_copy(rrowS[0:1, 0:512], k.ps[6][0:1, :]).then_inc(s, 1)
                v.tensor_copy(rrowS[0:1, 512:1024], k.ps[7][0:1, :]).then_inc(s, 1)
            p.add("vector", vrkB)
            ve.bump()
            c_rkB = ve.bump()
            p.tw("sync", (ve, c_rkB))
            rB = [p.dma("sync", k.RNKD[sl:sl + 1, 1024:2048], rrowS[0:1, :])]
            p.tw("scalar", rA + rB + (slot_ready[0] if sl == 1 else []))
            for t16 in range(16):
                p.dma("scalar", rnkc[:, t16:t16 + 1],
                      k.RNKD[sl:sl + 1, t16 * 128:(t16 + 1) * 128]
                      .rearrange("a n -> n a"))
            rk_toks = p.qdrain("scalar")
            p.tw("vector", rk_toks + [t4])
            def vslc(v, sl=sl, base=ve.n, s=ve.h):
                v.scalar_tensor_tensor(tm16a[:],
                                       colsr[:, :, 1 + 2 * sl:2 + 2 * sl]
                                       .rearrange("p t a -> p (t a)"), float(CAP),
                                       rnkc[:], ALU.mult, ALU.add).then_inc(s, 1)
                v.tensor_scalar(tm16b[:], rnkc[:], float(CAP) - 0.5, None,
                                ALU.is_gt).then_inc(s, 1)
                v.wait_ge(s, base + 2)
                v.tensor_scalar(rnkc[:], tm16a[:], dmpc[:, 0:1], None,
                                ALU.subtract).then_inc(s, 1)
                v.wait_ge(s, base + 3)
                v.tensor_tensor(rnkc[:], rnkc[:], tm16b[:], ALU.mult).then_inc(s, 1)
                v.wait_ge(s, base + 4)
                v.tensor_tensor(tm16a[:], tm16a[:], rnkc[:],
                                ALU.subtract).then_inc(s, 1)
                v.wait_ge(s, base + 5)
                v.tensor_scalar(tm16a[:], tm16a[:], 0.0, float(NE * CAP + 127),
                                ALU.max, ALU.min).then_inc(s, 1)
                v.tensor_scalar(tm16c[:], tokc[:], trsc[:, 0:1], None,
                                ALU.subtract).then_inc(s, 1)
                v.wait_ge(s, base + 7)
                v.tensor_copy(slotc[sl][:, :], tm16a[:]).then_inc(s, 1)
                v.tensor_tensor(tm16c[:], tm16c[:], tm16b[:],
                                ALU.mult).then_inc(s, 1)
                v.wait_ge(s, base + 9)
                v.scalar_tensor_tensor(tm16c[:], tm16c[:], -1.0, tokc[:],
                                       ALU.mult, ALU.add).then_inc(s, 1)
                v.wait_ge(s, base + 10)
                v.tensor_copy(slott[:, sl, :], tm16c[:]).then_inc(s, 1)
            p.add("vector", vslc)
            ve.bump(10)
            c_ur = ve.bump()
            slot_ready.append([(ve, c_ur)])
            p.tw("gpsimd", (ve, c_ur))
            for t16 in range(16):
                sem, cnt = p.slot("gpsimd")
                def gstk(g, sl=sl, t16=t16, s=sem.h):
                    g.indirect_dma_start(
                        out=k.SLOTTOK,
                        out_offset=IndirectOffsetOnAxis(slotc[sl][:, t16:t16 + 1], 0),
                        in_=slott[:, sl, t16:t16 + 1], in_offset=None,
                    ).then_inc(s, 16)
                p.add("gpsimd", gstk)
                sltk_toks.append((sem, cnt))
            emit_groups(4)
        k.slot_ready = slot_ready[0] + slot_ready[1]
        k.sltk_done = sltk_toks
        if RDBG:
            p.tw("sync", (ve, ve.n))
            p.dma("sync", k.RDB[:, 0:128], exall[:])
            p.dma("sync", k.RDB[:, 128:256], pall[:])
            p.dma("sync", k.RDB[:, 256:384],
                  colsr.ap().rearrange("p a b -> p (a b)"))
            p.dma("sync", k.RDB[:, 384:512], p2all[:])

        guard()
        prev_scat = {}
        prev_xsread = {}
        for sl in range(2):
            for t16 in range(16):
                pb = t16 % 2
                if ("xn2l", pb) in prev_xsread:
                    p.tw("sync", prev_xsread[("xn2l", pb)])
                c_xl = p.dma("sync", xn2l[pb][:], k.XT2[t16 * 128:(t16 + 1) * 128, :])
                if ("xs0", pb) in prev_scat:
                    p.tw("vector", prev_scat[("xs0", pb)])
                p.tw("vector", [c_xl] + slot_ready[sl])
                def vxs(v, sl=sl, t16=t16, pb=pb, s=ve.h):
                    v.tensor_scalar(xs0[pb][:], xn2l[pb][:],
                                    colsr[:, t16, 6 + sl:7 + sl], None,
                                    ALU.mult).then_inc(s, 1)
                p.add("vector", vxs)
                c_xs = ve.bump()
                prev_xsread[("xn2l", pb)] = (ve, c_xs)
                p.tw("gpsimd", [(ve, c_xs)] + (k.xsc_zero if sl == 0 and t16 == 0 else []))
                sem, cnt = p.slot("gpsimd")
                def gsc(g, sl=sl, t16=t16, pb=pb, s=sem.h):
                    g.indirect_dma_start(
                        out=k.XSC,
                        out_offset=IndirectOffsetOnAxis(slotc[sl][:, t16:t16 + 1], 0),
                        in_=xs0[pb][:], in_offset=None,
                    ).then_inc(s, 16)
                p.add("gpsimd", gsc)
                prev_scat[("xs0", pb)] = (sem, cnt)
                emit_groups(2)
        emit_groups(64)
        k.scat_done = p.qdrain("gpsimd")
        k.yzs_done = yzs_toks
        k.yp_done = sh["prev_yz"] or []

        p.emit()


def phase4c(k):
    nc = k.nc
    te, ve, se, ge = k.te, k.ve, k.se, k.ge
    xt2T = k.xt2T

    k.e4b.close()

    with ExitStack() as ectx:
        E = ectx.enter_context
        xeT = [E(nc.sbuf_tensor(f"p4xe{i}", [128, 8, CAP], BF16)) for i in range(2)]
        w1s = E(nc.sbuf_tensor("p4w1", [128, 8, D], BF16))
        w2s = E(nc.sbuf_tensor("p4w2", [128, 8, D], BF16))
        w3s = E(nc.sbuf_tensor("p4w3", [128, 8, D], BF16))
        hsl = E(nc.sbuf_tensor("p4h", [128, 8, CAP], BF16))
        b23 = E(nc.sbuf_tensor("p4b23", [1, 2, D], BF16))
        x3s = E(nc.sbuf_tensor("p4x3", [128, D], F32))
        prod = [E(nc.sbuf_tensor(f"p4pr{i}", [128, D], F32)) for i in range(8)]
        stokb = [E(nc.sbuf_tensor(f"p4sk{i}", [128, 1], U32)) for i in range(8)]

        p = Plan(k)
        p.tw("scalar", k.p0_done)
        p.tw("scalar", k.sltk_done)
        p.tw("tensor", k.yp_done)
        te_4b_end = te.n
        prev_te_w = te_4b_end
        prev_te_xe = {0: te_4b_end, 1: te_4b_end}
        prev_pr = 0
        prev_sadd = {i: None for i in range(8)}
        pend_sadd = []
        sadd_init = []

        def flush_sadd():
            if pend_sadd and not sadd_init:
                p.tw("gpsimd", k.yzs_done)
                sadd_init.append(1)
            for c_pr, c_stk, sb in pend_sadd:
                p.tw("gpsimd", [(ve, c_pr), c_stk])
                sem, cnt = p.slot("gpsimd")
                def gsa(g, sb=sb, s=sem.h):
                    g.indirect_dma_start(
                        out=k.YZS,
                        out_offset=IndirectOffsetOnAxis(stokb[sb][:, 0:1], 0),
                        in_=prod[sb][:], in_offset=None, compute_op=ALU.add,
                    ).then_inc(s, 16)
                p.add("gpsimd", gsa)
                p.tw("gpsimd", (sem, cnt))
                prev_sadd[sb] = (sem, cnt)
            pend_sadd.clear()
        for e in range(NE):
            pb = e % 2
            p.tw("gpsimd", (te, prev_te_w))
            cw1 = p.dma("gpsimd", w1s.ap(), k.w1T[e].rearrange("(c p) h -> p c h", p=128))
            cw2 = p.dma("gpsimd", w2s.ap(), k.w2T[e].rearrange("(c p) h -> p c h", p=128))
            cw3 = p.dma("gpsimd", w3s.ap(), k.w3T[e].rearrange("(c p) h -> p c h", p=128))
            cb2 = p.dma("gpsimd", b23[0:1, 0, :], k.b2r[e:e + 1, :])
            cb3 = p.dma("gpsimd", b23[0:1, 1, :], k.b3r[e:e + 1, :])
            flush_sadd()
            p.tw("scalar", (te, prev_te_xe[pb]))
            if e == 0:
                p.tw("scalar", k.scat_done)
            for dc in range(8):
                p.dma("scalar", xeT[pb][:, dc, :],
                      k.XSC[e * CAP:(e + 1) * CAP, dc * 128:(dc + 1) * 128],
                      transpose=True)
            xe_toks = p.qdrain("scalar")

            u_done = {}
            p.tw("tensor", [cw1] + xe_toks)
            for ht in range(8):
                for hf in range(2):
                    idx = ht * 2 + hf
                    ud2 = u_done.get(idx - 2, 0)
                    def mmu(t, pb=pb, ht=ht, hf=hf, idx=idx, ud2=ud2, s=te.h):
                        if idx >= 2:
                            t.wait_ge(se.h, ud2)
                        for dc in range(8):
                            ins = t.matmul(up_[hf][:, :],
                                           w1s[:, dc, ht * 128:(ht + 1) * 128],
                                           xeT[pb][:, dc, hf * 512:(hf + 1) * 512],
                                           start=(dc == 0), stop=(dc == 7))
                        ins.then_inc(s, 1)
                    up_ = [k.ps[4], k.ps[5]]
                    p.add("tensor", mmu)
                    c_mu = te.bump()
                    def su(s_, e=e, ht=ht, hf=hf, c=c_mu, s=se.h):
                        s_.wait_ge(te.h, c)
                        s_.activation(hsl[:, ht, hf * 512:(hf + 1) * 512],
                                      k.ps[4 + hf][:, :], AF.Silu,
                                      bias=k.b1cs[:, e, ht:ht + 1]).then_inc(s, 1)
                    p.add("scalar", su)
                    u_done[idx] = se.bump()
            x23p = [k.ps[6], k.ps[7]]
            for st in range(8):
                if st == 0:
                    p.tw("tensor", [cw3, cb3, (se, u_done[15]), k.ones_ready])
                for half in range(2):
                    def mm3(t, pb=pb, st=st, half=half, c4=prev_pr, s=te.h):
                        if c4 > 0:
                            t.wait_ge(ve.h, c4)
                        for dc in range(8):
                            t.matmul(x23p[half][:, :],
                                     xeT[pb][:, dc, st * 128:(st + 1) * 128],
                                     w3s[:, dc, half * 512:(half + 1) * 512],
                                     start=(dc == 0), stop=False)
                        t.matmul(x23p[half][:, :], k.onesr[0:1, :],
                                 b23[0:1, 1, half * 512:(half + 1) * 512],
                                 start=False, stop=True).then_inc(s, 1)
                    p.add("tensor", mm3)
                    te.bump()
                c_m3 = te.n
                def v3(v, c=c_m3, s=ve.h):
                    v.wait_ge(te.h, c)
                    v.tensor_copy(x3s[:, 0:512], x23p[0][:, :]).then_inc(s, 1)
                    v.tensor_copy(x3s[:, 512:1024], x23p[1][:, :]).then_inc(s, 1)
                p.add("vector", v3)
                ve.bump()
                c_v3 = ve.bump()
                if st == 0:
                    p.tw("tensor", [cw2, cb2])
                for half in range(2):
                    def mm2(t, pb=pb, st=st, half=half, c3=c_v3, s=te.h):
                        t.wait_ge(ve.h, c3)
                        for hc in range(8):
                            t.matmul(x23p[half][:, :],
                                     hsl[:, hc, st * 128:(st + 1) * 128],
                                     w2s[:, hc, half * 512:(half + 1) * 512],
                                     start=(hc == 0), stop=False)
                        t.matmul(x23p[half][:, :], k.onesr[0:1, :],
                                 b23[0:1, 0, half * 512:(half + 1) * 512],
                                 start=False, stop=True).then_inc(s, 1)
                    p.add("tensor", mm2)
                    te.bump()
                c_m2 = te.n
                sb = st
                if prev_sadd[sb] is not None:
                    p.tw("scalar", prev_sadd[sb])
                c_stk = p.dma("scalar", stokb[sb][:],
                              k.SLOTTOK[e * CAP + st * 128:e * CAP + (st + 1) * 128, :])
                if prev_sadd[sb] is not None:
                    p.tw("vector", prev_sadd[sb])
                def vpr(v, c=c_m2, sb=sb, s=ve.h):
                    v.wait_ge(te.h, c)
                    v.tensor_tensor(prod[sb][:, 0:512], x23p[0][:, :], x3s[:, 0:512],
                                    ALU.mult).then_inc(s, 1)
                    v.tensor_tensor(prod[sb][:, 512:1024], x23p[1][:, :],
                                    x3s[:, 512:1024], ALU.mult).then_inc(s, 1)
                p.add("vector", vpr)
                ve.bump()
                c_pr = ve.bump()
                prev_pr = c_pr
                pend_sadd.append((c_pr, c_stk, sb))
            prev_te_w = te.n
            prev_te_xe[pb] = te.n
        flush_sadd()
        k.ysc_done = p.qdrain("gpsimd")
        p.emit()


def phase4d(k):
    nc = k.nc
    te, ve, se, ge = k.te, k.ve, k.se, k.ge
    slotc = k.slotc
    with ExitStack() as ectx:
        E = ectx.enter_context
        yt = [E(nc.sbuf_tensor(f"p4dy{i}", [128, D], F32)) for i in range(2)]
        xot2 = [E(nc.sbuf_tensor(f"p4xo2{i}", [128, D], F32)) for i in range(2)]
        cols5 = E(nc.sbuf_tensor("p4cols5", [128, 8], F32))
        sqd5 = E(nc.sbuf_tensor("p4sqd5", [128, D], BF16))
        ln4f = E(nc.sbuf_tensor("p4ln4", [128, D], F32))
        outt = [E(nc.sbuf_tensor(f"p4ot{i}", [128, D], F32)) for i in range(2)]

        p = Plan(k)
        p.tw("sync", k.yzs_done)
        p.tw("sync", k.ysc_done)
        out_toks = []
        prev_lf = {}
        for t16 in range(16):
            pb = t16 % 2
            if pb in prev_lf:
                p.tw("sync", (se, prev_lf[pb]))
            c_y = p.dma("sync", yt[pb][:], k.YZS[t16 * 128:(t16 + 1) * 128, :])
            c_xo = p.dma("sync", xot2[pb][:], k.X2S[t16 * 128:(t16 + 1) * 128, :])
            p.tw("vector", c_y)
            p.tw("scalar", c_y)
            rstd, nmr, c_c5 = ln_cols(k, p, yt[pb][:], cols5[:], 4 * pb, sqd5[:])
            def lf(s_, pb=pb, sc=rstd, bi=nmr, c=c_c5, s=se.h):
                s_.wait_ge(ve.h, c)
                s_.activation(ln4f[:], yt[pb][:], AF.Identity, bias=bi,
                              scale=sc).then_inc(s, 1)
            p.add("scalar", lf)
            c_lf = se.bump()
            prev_lf[pb] = c_lf
            p.tw("vector", c_xo)
            def vo(v, pb=pb, c1=c_lf, s=ve.h):
                v.wait_ge(se.h, c1)
                v.tensor_tensor(outt[pb][:], xot2[pb][:], ln4f[:], ALU.add).then_inc(s, 1)
            p.add("vector", vo)
            c_o = ve.bump()
            p.tw("scalar", (ve, c_o))
            p.tw("sync", (ve, c_o))
            stc = p.dma("sync", k.out[t16 * 128:(t16 + 1) * 128, :], outt[pb][:])
            out_toks.append(stc)
            p.tw("vector", stc)
        p.tw("sync", p.qdrain("sync"))
        p.emit()


def _host_tables(r):
    hh = np.arange(4 * r, 4 * r + 4)
    g = np.log1p(-(2.0 ** (-5.0 - hh.astype(np.float64))))
    t = np.arange(T, dtype=np.float64)
    scale = np.sqrt((1.0 - np.exp(g[:, None] * (t[None, :] + 1.0)))
                    / (1.0 - np.exp(g))[:, None])
    qrow = np.exp(g[:, None] * (t[None, :] % 512)) / scale
    kcol = np.exp(-g[:, None] * (t[None, :] % 128))
    qrowt = np.zeros((2, 128, T))
    kcolt = np.zeros((2, 128, T))
    for ft in range(2):
        qrowt[ft, 0:64, :] = qrow[2 * ft]
        qrowt[ft, 64:128, :] = qrow[2 * ft + 1]
        kcolt[ft, 0:64, :] = kcol[2 * ft]
        kcolt[ft, 64:128, :] = kcol[2 * ft + 1]
    cols = []
    for pair in range(2):
        for h in range(2):
            gh = g[2 * pair + h]
            for qc in range(16):
                for kt in range(4 * qc + 4):
                    cols.append(np.exp(gh * (512.0 * qc - 128.0 * kt)))
    cft = np.tile(np.asarray(cols)[None, :], (128, 1))
    tri4 = np.zeros((4, 128, 512))
    sl = np.arange(128)
    tl = np.arange(512)
    for dk in range(4):
        tri4[dk] = (tl[None, :] >= 128 * dk + sl[:, None]).astype(np.float64)
    return (qrowt.astype(ml_dtypes.bfloat16), kcolt.astype(ml_dtypes.bfloat16),
            cft.astype(np.float32), tri4.astype(ml_dtypes.bfloat16))


def kernel(L0_attn_x, L0_attn_W_qk, L0_attn_b_qk, L0_attn_W_v, L0_attn_b_v,
           L0_attn_W_g, L0_attn_b_g, L0_attn_W_o, L0_attn_b_o,
           L0_ffn_gate_w, L0_ffn_w1, L0_ffn_b1, L0_ffn_w2, L0_ffn_b2,
           L0_ffn_w3, L0_ffn_b3, L0_ffn_sw1, L0_ffn_sw2, L0_ffn_sw3):
    x = np.ascontiguousarray(np.asarray(L0_attn_x))
    W_qk = np.asarray(L0_attn_W_qk)
    W_v = np.asarray(L0_attn_W_v)
    W_g = np.asarray(L0_attn_W_g)
    W_o = np.ascontiguousarray(np.asarray(L0_attn_W_o))
    b_qk = np.asarray(L0_attn_b_qk)
    b_v = np.asarray(L0_attn_b_v)
    b_g = np.asarray(L0_attn_b_g)
    b_o = np.ascontiguousarray(np.asarray(L0_attn_b_o).reshape(1, D))
    gwT = np.ascontiguousarray(np.asarray(L0_ffn_gate_w).T)
    w1T = np.ascontiguousarray(np.asarray(L0_ffn_w1).transpose(0, 2, 1))
    w2T = np.ascontiguousarray(np.asarray(L0_ffn_w2).transpose(0, 2, 1))
    w3T = np.ascontiguousarray(np.asarray(L0_ffn_w3).transpose(0, 2, 1))
    b1c = np.ascontiguousarray(np.asarray(L0_ffn_b1).reshape(NE, 8, 128).transpose(0, 2, 1))
    b2r = np.ascontiguousarray(np.asarray(L0_ffn_b2))
    b3r = np.ascontiguousarray(np.asarray(L0_ffn_b3))
    sw1T = np.ascontiguousarray(np.asarray(L0_ffn_sw1).T)
    sw2T = np.ascontiguousarray(np.asarray(L0_ffn_sw2).T)
    sw3T = np.ascontiguousarray(np.asarray(L0_ffn_sw3).T)

    nc = build_nc()
    in_maps = []
    for c in range(8):
        b, r = c // 4, c % 4
        qrowt, kcolt, cft, tri4 = _host_tables(r)
        pp = np.arange(128)[:, None]
        vt = np.arange(16)[None, :]
        ccidx = ((vt * 128 + pp) * 4 + r).astype(np.uint32)
        m = dict(
            xb=np.ascontiguousarray(x[b]),
            xown=np.ascontiguousarray(x[b, OWN * r:OWN * (r + 1)]),
            wq=np.ascontiguousarray(W_qk[:, 256 * r:256 * (r + 1)]),
            wk=np.ascontiguousarray(W_qk[:, D + 256 * r:D + 256 * (r + 1)]),
            wv=np.ascontiguousarray(W_v[:, 512 * r:512 * (r + 1)]),
            wg=np.ascontiguousarray(W_g[:, 512 * r:512 * (r + 1)]),
            bqc=np.ascontiguousarray(b_qk[256 * r:256 * (r + 1)].reshape(2, 128).T),
            bkc=np.ascontiguousarray(b_qk[D + 256 * r:D + 256 * (r + 1)].reshape(2, 128).T),
            bgc=np.ascontiguousarray(b_g[512 * r:512 * (r + 1)].reshape(4, 128).T),
            bvr=np.ascontiguousarray(b_v[512 * r:512 * (r + 1)].reshape(1, 512)),
            wo=W_o, bor=b_o, gwT=gwT,
            w1T=w1T, w2T=w2T, w3T=w3T, b1c=b1c, b2r=b2r, b3r=b3r,
            sw1T=sw1T, sw2T=sw2T, sw3T=sw3T,
            qrowt=qrowt, kcolt=kcolt, cft=cft, tri4=tri4, ccidx=ccidx,
            iot6=np.tile(1e6 + np.arange(8, dtype=np.float64), (128, 1)).astype(np.float32),
            iot0=np.tile(np.arange(8, dtype=np.float64), (128, 1)).astype(np.float32),
            eioc=np.arange(8, dtype=np.float32).reshape(8, 1),
            dmpp=(NE * CAP + np.arange(128)).astype(np.float32).reshape(128, 1),
            tokf=(np.arange(16)[None, :] * 128 + np.arange(128)[:, None]).astype(np.float32),
            trsf=(2048 + np.arange(128)).astype(np.float32).reshape(128, 1),
            sltk=(2048 + (np.arange(NE * CAP + 128) % 128)).astype(np.uint32).reshape(-1, 1),
        )
        in_maps.append(m)

    res = run_bass_kernel_spmd(nc, in_maps, core_ids=list(range(8)))
    out = np.empty((2, T, D), dtype=np.float32)
    for c in range(8):
        b, r = c // 4, c % 4
        out[b, OWN * r:OWN * (r + 1)] = res.results[c]["out"]
    return out
